# version 34
# baseline (speedup 1.0000x reference)
import numpy as np
from contextlib import ExitStack

import concourse.bass as bass
import concourse.mybir as mybir
from concourse.bass_utils import run_bass_kernel_spmd

F32 = mybir.dt.float32
I32 = mybir.dt.int32
BF16 = mybir.dt.bfloat16
AF = mybir.ActivationFunctionType
ALU = mybir.AluOpType

B, T, D, DS, H = 8, 4096, 1024, 64, 3072
SUB = 128
CHUNK = 512
NSUB = CHUNK // SUB
ND = D // 128
NH = H // 128
EPS = 1e-5
QSC = 1.0 / 127.0


class Prog:

    ENGINES = ("sync", "gpsimd", "tensor", "vector", "scalar")
    NSLOTS = 8

    def __init__(self):
        self.ops = []
        self.ndma = {}

    def add(self, eng, fn, deps=(), dma=False, tag=""):
        deps = [d for d in deps if d is not None]
        op = {"eng": eng, "fn": fn, "deps": deps, "dma": dma,
              "idx": len(self.ops), "tag": tag, "cum": None, "need_inc": dma,
              "slot": None, "prewait": None}
        if dma:
            i = self.ndma.get(eng, 0)
            self.ndma[eng] = i + 1
            op["slot"] = i % self.NSLOTS
        for d in deps:
            assert d["idx"] < op["idx"], (tag, d["tag"])
        self.ops.append(op)
        return op

    def emit(self, block, sems):
        for op in self.ops:
            if op["eng"] in ("vector", "scalar") or (op["eng"] == "gpsimd" and not op["dma"]):
                op["need_inc"] = True
            for d in op["deps"]:
                if d["eng"] != op["eng"] or d["dma"]:
                    d["need_inc"] = True
        counters = {}
        slot_prev = {}
        for op in self.ops:
            if op["dma"]:
                key = (op["eng"], "d", op["slot"])
                op["prewait"] = slot_prev.get(key)
                counters[key] = counters.get(key, 0) + 16
                op["cum"] = counters[key]
                op["sem_key"] = key
                slot_prev[key] = op
            elif op["need_inc"]:
                key = (op["eng"], "c")
                counters[key] = counters.get(key, 0) + 1
                op["cum"] = counters[key]
                op["sem_key"] = key

        def run_engine(eng_name):
            chain = eng_name in ("vector", "scalar", "gpsimd")
            def body(eng):
                waited = {}
                prev = None
                def wait_for(d):
                    sk, v = d["sem_key"], d["cum"]
                    if waited.get(sk, -1) >= v:
                        return
                    eng.wait_ge(sems[sk], v)
                    waited[sk] = v
                for op in self.ops:
                    if op["eng"] != eng_name:
                        continue
                    for d in op["deps"]:
                        if d["eng"] == eng_name and not d["dma"] and not chain:
                            continue
                        wait_for(d)
                    if chain and prev is not None and not op["dma"]:
                        wait_for(prev)
                    if op["prewait"] is not None:
                        wait_for(op["prewait"])
                    if chain and not op["dma"]:
                        prev = op
                    instr = op["fn"](eng)
                    if op["need_inc"]:
                        instr.then_inc(sems[op["sem_key"]],
                                       16 if op["dma"] else 1)
            return body

        block.sync(run_engine("sync"))
        block.gpsimd(run_engine("gpsimd"))
        block.tensor(run_engine("tensor"))
        block.vector(run_engine("vector"))
        block.scalar(run_engine("scalar"))


def build(T_core):
    assert T_core % CHUNK == 0
    NCH = T_core // CHUNK
    nc = bass.Bass("TRN2", target_bir_lowering=False)

    x_h = nc.dram_tensor("x", [SUB, T_core // SUB, D], F32, kind="ExternalInput").ap()
    wqT_h = nc.dram_tensor("wqT", [128, ND * DS], F32, kind="ExternalInput").ap()
    wkT_h = nc.dram_tensor("wkT", [128, ND * DS], F32, kind="ExternalInput").ap()
    wvT_h = nc.dram_tensor("wvT", [128, ND * DS], F32, kind="ExternalInput").ap()
    wo_h = nc.dram_tensor("wo_pad", [DS + 1, D], F32, kind="ExternalInput").ap()
    bq_h = nc.dram_tensor("bq", [DS], F32, kind="ExternalInput").ap()
    bk_h = nc.dram_tensor("bk", [DS], F32, kind="ExternalInput").ap()
    bv_h = nc.dram_tensor("bv", [DS], F32, kind="ExternalInput").ap()
    dec_h = nc.dram_tensor("dec", [DS], F32, kind="ExternalInput").ap()
    wgT_h = nc.dram_tensor("wgT", [NH // 2, 128, 16 * 128], I32, kind="ExternalInput").ap()
    wuT_h = nc.dram_tensor("wuT", [NH // 2, 128, 16 * 128], I32, kind="ExternalInput").ap()
    wdT_h = nc.dram_tensor("wdT", [2, NH // 2, 128, 1024], I32, kind="ExternalInput").ap()
    wg_s = nc.dram_tensor("wg_s", [NH // 2, 128, 16 * 128], BF16, kind="Internal").ap()
    wu_s = nc.dram_tensor("wu_s", [NH // 2, 128, 16 * 128], BF16, kind="Internal").ap()
    wd_s = nc.dram_tensor("wd_s", [2, NH // 2, 128, 1024], BF16, kind="Internal").ap()
    mask_h = nc.dram_tensor("maskT", [128, CHUNK], F32, kind="ExternalInput").ap()
    idn_h = nc.dram_tensor("ident", [128, 128], F32, kind="ExternalInput").ap()
    out_h = nc.dram_tensor("out", [SUB, T_core // SUB, D], F32, kind="ExternalOutput").ap()

    p = Prog()
    es = ExitStack()
    with es:
        sb = lambda name, shape, dt=F32: es.enter_context(
            nc.sbuf_tensor(name, shape, dt))
        xA = sb("xA", [128, NSUB * D])
        xB = sb("xB", [128, NSUB * D])
        xTh = sb("xTh", [128, ND * CHUNK], BF16)
        xTl = sb("xTl", [128, ND * CHUNK], BF16)
        x1 = sb("x1", [128, 2 * NSUB * D])
        x1Th = sb("x1Th", [128, ND * CHUNK], BF16)
        x1Tl = sb("x1Tl", [128, ND * CHUNK], BF16)
        hTh = sb("hTh", [128, NH * CHUNK], BF16)
        hTl = sb("hTl", [128, NH * CHUNK], BF16)
        htmp = sb("htmp", [128, CHUNK])
        sg = sb("sg", [128, CHUNK])
        qT = sb("qT", [64, CHUNK])
        kT = sb("kT", [64, CHUNK])
        vt = sb("vt", [128, NSUB * 64])
        ktok4 = sb("ktok4", [128, NSUB * 64])
        at4 = sb("at4", [128, CHUNK])
        aoTh = sb("aoTh", [65, CHUNK], BF16)
        aoTl = sb("aoTl", [65, CHUNK], BF16)
        stt2 = sb("stt2", [64, 128])
        tsum = sb("tsum", [64, 192])
        stats = sb("stats", [128, 64])
        wqh = sb("wqh", [128, ND * DS], BF16)
        wql = sb("wql", [128, ND * DS], BF16)
        wkh = sb("wkh", [128, ND * DS], BF16)
        wkl = sb("wkl", [128, ND * DS], BF16)
        wvh = sb("wvh", [128, ND * DS], BF16)
        wvl = sb("wvl", [128, ND * DS], BF16)
        woh = sb("woh", [65, D], BF16)
        wol = sb("wol", [65, D], BF16)
        wg = sb("wg", [128, 4 * D], BF16)
        wu = sb("wu", [128, 4 * D], BF16)
        wd = sb("wd", [128, 2 * 1024], BF16)
        junk = sb("junk", [128, D], BF16)
        bq_sb = sb("bq_sb", [64, 1])
        bk_sb = sb("bk_sb", [64, 1])
        bv_row = sb("bv_row", [1, 64])
        dinv4 = sb("dinv4", [128, NSUB * 64])
        d0t4 = sb("d0t4", [64, 128])
        dinvT = x1
        d0ext = at4
        rowC = sb("rowC", [64, 64])
        mask = sb("mask", [128, CHUNK])
        idn = sb("idn", [128, 128])
        ones = mask
        PS = [es.enter_context(nc.psum_tensor("pb%d" % i, [128, 1024], F32))
              for i in range(4)]
        sems = {}
        for e in Prog.ENGINES:
            sems[(e, "c")] = es.enter_context(nc.semaphore("sem_%s_c" % e))
        for e in ("sync", "gpsimd"):
            for sl in range(Prog.NSLOTS):
                sems[(e, "d", sl)] = es.enter_context(
                    nc.semaphore("sem_%s_d%d" % (e, sl)))
        block = es.enter_context(nc.Block())

        nc.const_aps.register(F32, float(QSC), stats[0:1, 63:64])
        bankR = {}

        def dma(eng, out_ap, in_ap, deps=(), tag=""):
            return p.add(eng, lambda e, o=out_ap, i=in_ap: e.dma_start(out=o, in_=i),
                         deps=deps, dma=True, tag=tag)

        ld_idn = dma("sync", idn[:, :], idn_h, tag="ld_idn")
        ld_dec = dma("sync", ktok4[0:1, 131:195], dec_h[None, :], tag="ld_dec")
        ld_bq = dma("sync", bq_sb[:, :], bq_h[:, None], tag="ld_bq")
        ld_bk = dma("sync", bk_sb[:, :], bk_h[:, None], tag="ld_bk")
        ld_bv = dma("sync", bv_row[:, :], bv_h[None, :], tag="ld_bv")
        ld_mask = dma("sync", mask[:, :], mask_h, tag="ld_mask")
        x_load_ops = {}
        for n in range(min(2, NCH)):
            buf = xA if n % 2 == 0 else xB
            x_load_ops[n] = dma(
                "gpsimd", buf.ap().rearrange("p (s d) -> p s d", s=NSUB),
                x_h[:, n * NSUB:(n + 1) * NSUB, :], tag="ldx%d" % n)
        def split_w(name, hbuf, lbuf, rows, cols, off, load_fn):
            stage = x1[0:rows, off:off + cols]
            ld = load_fn(stage)
            hi = p.add("scalar", lambda e: e.copy(hbuf[0:rows, 0:cols], stage),
                       deps=[ld], tag="sw_h_" + name)
            lo = p.add("vector",
                       lambda e: e.scalar_tensor_tensor(
                           lbuf[0:rows, 0:cols], stage, 1.0, hbuf[0:rows, 0:cols],
                           ALU.mult, ALU.subtract),
                       deps=[hi], tag="sw_l_" + name)
            return lo


        store_ops = []
        x1_free = {}
        res1_done = {}
        gate_par = {}
        up_par = {}
        wd_par = {}
        prev_down_by_hc = {}
        prev_hmul = {}
        prev_hh = {}
        hh_by = {}
        hlo_by = {}
        prev_mm_i2 = None
        prev_stn = None
        xT_ready = None
        x1T_done = {}
        gs_box = [0]
        SB1 = lambda s: s * 7
        SB2 = lambda s: (NSUB + s) * 7

        def x1p(n, s):
            return x1[:, ((n % 2) * NSUB + s) * D:((n % 2) * NSUB + s + 1) * D]

        def x1ph(n, s, hh):
            base = ((n % 2) * NSUB + s) * D + hh * 512
            return x1[:, base:base + 512]

        cv_wg_a = dma("gpsimd", wg_s[0:3], wgT_h[0:3], tag="cv_wg_a")
        cv_wu_a = dma("gpsimd", wu_s[0:3], wuT_h[0:3], tag="cv_wu_a")
        cv_wg_b = dma("gpsimd", wg_s[3:], wgT_h[3:], tag="cv_wg_b")
        cv_wu_b = dma("gpsimd", wu_s[3:], wuT_h[3:], tag="cv_wu_b")
        cv_wd = dma("gpsimd", wd_s, wdT_h, tag="cv_wd")

        def emit_trx(n, s, pi=3):
            nbuf = xA if n % 2 == 0 else xB
            tr = None
            for dc in range(ND):
                dep = [x_load_ops[n], ld_idn,
                       bankR.get(2 * pi), bankR.get(2 * pi + 1)] if dc == 0 else []
                tr = p.add("tensor",
                           lambda e, s=s, dc=dc, nbuf=nbuf, pi=pi: e.transpose(
                               PS[pi][:, dc * 128:(dc + 1) * 128],
                               nbuf[:, s * D + dc * 128:s * D + (dc + 1) * 128],
                               idn[:, :]),
                           deps=dep, tag="trx%d_%d_%d" % (n, s, dc))
            evx = p.add("scalar",
                        lambda e, s=s, pi=pi: e.copy(
                            xTh.ap().rearrange("p (dc t) -> p dc t", dc=ND)[:, :, s * 128:(s + 1) * 128],
                            PS[pi].ap().rearrange("p (dc t) -> p dc t", t=128)),
                        deps=[tr], tag="evx%d_%d" % (n, s))
            evxl = p.add("vector",
                         lambda e, s=s, pi=pi: e.scalar_tensor_tensor(
                             xTl.ap().rearrange("p (dc t) -> p dc t", dc=ND)[:, :, s * 128:(s + 1) * 128],
                             PS[pi].ap().rearrange("p (dc t) -> p dc t", t=128),
                             1.0,
                             xTh.ap().rearrange("p (dc t) -> p dc t", dc=ND)[:, :, s * 128:(s + 1) * 128],
                             ALU.mult, ALU.subtract),
                         deps=[evx], tag="evxl%d_%d" % (n, s))
            bankR[2 * pi] = evxl
            bankR[2 * pi + 1] = evxl
            return evxl

        def stage_attn(n):
            st = {}
            nbuf = xA if n % 2 == 0 else xB

            def _proj3(psl, wh, wl, dep0, tagp):
                mm = None
                k = 0
                for dc in range(ND):
                    for wsrc, xsrc in ((wh, xTh), (wh, xTl), (wl, xTh)):
                        dep = dep0 if k == 0 else []
                        mm = p.add("tensor",
                                   lambda e, dc=dc, wsrc=wsrc, xsrc=xsrc, psl=psl, k=k: e.matmul(
                                       psl, wsrc[:, dc * 64:(dc + 1) * 64],
                                       xsrc[:, dc * CHUNK:(dc + 1) * CHUNK],
                                       start=(k == 0), stop=(k == 3 * ND - 1)),
                                   deps=dep, tag=tagp + "_%d" % k)
                        k += 1
                return mm

            def t_q():
                mm = _proj3(PS[2][0:64, 0:512], wqh, wql,
                            [xT_ready, sw_wq, ld_bq, bankR.get(4)], "mmq%d" % n)
                st["ev_q"] = p.add("vector",
                                   lambda e: e.tensor_scalar(qT[:, :], PS[2][0:64, 0:512],
                                                             bq_sb[:, 0:1], None, ALU.add),
                                   deps=[mm], tag="evq%d" % n)
                bankR[4] = st["ev_q"]

            def t_k():
                mm = _proj3(PS[2][0:64, 512:1024], wkh, wkl,
                            [xT_ready, sw_wk, ld_bk, bankR.get(5)], "mmk%d" % n)
                st["ev_k"] = p.add("vector",
                                   lambda e: e.tensor_scalar(kT[:, :], PS[2][0:64, 512:1024],
                                                             bk_sb[:, 0:1], None, ALU.add),
                                   deps=[mm], tag="evk%d" % n)
                bankR[5] = st["ev_k"]

            def t_v(lo):
                mm = st.get("mm_v")
                for s in range(lo, lo + 2):
                    for dc in range(ND):
                        for pi, (xsrc, wsrc) in enumerate(((xTh, wvh), (xTh, wvl), (xTl, wvh))):
                            dep = [xT_ready, sw_wv, ld_bv, bankR.get(6)] if (s == 0 and dc == 0 and pi == 0) else []
                            mm = p.add("tensor",
                                       lambda e, s=s, dc=dc, xsrc=xsrc, wsrc=wsrc, pi=pi: e.matmul(
                                           PS[3][:, s * 64:(s + 1) * 64],
                                           xsrc[:, dc * CHUNK + s * 128: dc * CHUNK + (s + 1) * 128],
                                           wsrc[:, dc * 64:(dc + 1) * 64],
                                           start=(s == 0 and dc == 0 and pi == 0), stop=False),
                                       deps=dep, tag="mmv%d_%d_%d_%d" % (n, s, dc, pi))
                    mm = p.add("tensor",
                               lambda e, s=s: e.matmul(
                                   PS[3][:, s * 64:(s + 1) * 64], ones[0:1, 0:128],
                                   bv_row[:, :], start=False, stop=(s == NSUB - 1)),
                               deps=[], tag="mmvb%d_%d" % (n, s))
                st["mm_v"] = mm
                if lo == 2:
                    st["ev_v"] = p.add("vector",
                                       lambda e: e.tensor_tensor(vt[:, :], PS[3][:, 0:256],
                                                                 dinv4[:, :], ALU.mult),
                                       deps=[mm] + ev_di, tag="evv%d" % n)
                    bankR[6] = st["ev_v"]

            def t_sc():
                mm = None
                for s in range(NSUB):
                    dep = [st["ev_q"], st["ev_k"], bankR.get(7)] if s == 0 else []
                    mm = p.add("tensor",
                               lambda e, s=s: e.matmul(
                                   PS[3][:, 512 + s * 128:512 + (s + 1) * 128],
                                   kT[:, s * 128:(s + 1) * 128],
                                   qT[:, s * 128:(s + 1) * 128],
                                   start=(s == 0), stop=(s == NSUB - 1)),
                               deps=dep, tag="mmsc%d_%d" % (n, s))
                st["ev_at4"] = p.add("vector",
                                     lambda e: e.tensor_tensor(at4[:, :], PS[3][:, 512:1024],
                                                               mask[:, :], ALU.mult),
                                     deps=[mm, ld_mask], tag="evat%d" % n)
                bankR[7] = st["ev_at4"]

            def t_kt():
                tr = None
                for s in range(NSUB):
                    dep = [st["ev_k"], st["ev_v"], bankR.get(6)] if s == 0 else []
                    tr = p.add("tensor",
                               lambda e, s=s: e.transpose(
                                   PS[3][:, 256 + s * 64:256 + (s + 1) * 64],
                                   kT[0:64, s * 128:(s + 1) * 128],
                                   idn[0:64, 0:64]),
                               deps=dep, tag="trk%d_%d" % (n, s))
                st["ev_kt4"] = p.add("vector",
                                     lambda e: e.tensor_copy(ktok4[:, :], PS[3][:, 256:512]),
                                     deps=[tr], tag="evkt%d" % n)
                bankR[6] = st["ev_kt4"]

            def t_scan(s):
                nonlocal prev_mm_i2, prev_stn
                gs = gs_box[0]
                cur = gs % 2
                nxt = (gs + 1) % 2
                gs_box[0] += 1
                mm_i1 = p.add("tensor",
                              lambda e, s=s: e.matmul(
                                  PS[2][0:64, s * 128:(s + 1) * 128],
                                  vt[:, s * 64:(s + 1) * 64],
                                  at4[:, s * 128:(s + 1) * 128],
                                  start=(s == 0), stop=False),
                              deps=[st["ev_v"], st["ev_at4"]] + ([bankR.get(4)] if s == 0 else []),
                              tag="mmi1%d_%d" % (n, s))
                mm_i2 = p.add("tensor",
                              lambda e, s=s, cur=cur: e.matmul(
                                  PS[2][0:64, s * 128:(s + 1) * 128],
                                  stt2[:, cur * 64:(cur + 1) * 64],
                                  qT[:, s * 128:(s + 1) * 128],
                                  start=False, stop=(s == NSUB - 1)),
                              deps=[prev_stn, ms_st], tag="mmi2%d_%d" % (n, s))
                mm_p = p.add("tensor",
                             lambda e, s=s: e.matmul(
                                 PS[2][0:64, 512:576], ktok4[:, s * 64:(s + 1) * 64],
                                 vt[:, s * 64:(s + 1) * 64], start=True, stop=True),
                             deps=[st["ev_kt4"], st.get("p_rd")] +
                                  ([bankR.get(5)] if s == 0 else []),
                             tag="mmp%d_%d" % (n, s))
                o_sum = p.add("vector",
                              lambda e, cur=cur: e.tensor_tensor(
                                  tsum[:, 0:64], stt2[:, cur * 64:(cur + 1) * 64],
                                  PS[2][0:64, 512:576], ALU.add),
                              deps=[mm_p, prev_stn, ms_st], tag="tsum%d_%d" % (n, s))
                st["p_rd"] = o_sum
                bankR[5] = o_sum
                o_stn = p.add("vector",
                              lambda e, nxt=nxt: e.tensor_tensor(
                                  stt2[:, nxt * 64:(nxt + 1) * 64], tsum[:, 0:64],
                                  rowC[:, :], ALU.mult),
                              deps=[bc_c, prev_mm_i2], tag="stn%d_%d" % (n, s))
                prev_mm_i2 = mm_i2
                prev_stn = o_stn
                st["ao_mm"] = mm_i2

            def t_ao():
                ev = None
                for s in range(NSUB):
                    t0 = p.add("vector",
                               lambda e, s=s: e.tensor_tensor(
                                   tsum[:, 64:192], PS[2][0:64, s * 128:(s + 1) * 128],
                                   d0t4[:, :], ALU.mult),
                               deps=[st["ao_mm"]] + ev_d04, tag="evao%d_%d" % (n, s))
                    th = p.add("scalar",
                               lambda e, s=s: e.copy(
                                   aoTh[0:64, s * 128:(s + 1) * 128], tsum[:, 64:192]),
                               deps=[t0], tag="evaoh%d_%d" % (n, s))
                    ev = p.add("vector",
                               lambda e, s=s: e.scalar_tensor_tensor(
                                   aoTl[0:64, s * 128:(s + 1) * 128],
                                   tsum[:, 64:192], 1.0,
                                   aoTh[0:64, s * 128:(s + 1) * 128],
                                   ALU.mult, ALU.subtract),
                               deps=[th], tag="evaol%d_%d" % (n, s))
                st["ev_ao"] = ev
                bankR[4] = ev

            def t_proj(s):
                mm0 = mm1 = None
                for hh in range(2):
                    first = True
                    for asrc, wsrc in ((aoTh, woh), (aoTh, wol), (aoTl, woh)):
                        if hh == 0 and first:
                            dep = [st["ev_ao"], sw_wo, ms_ao1, ms_ao0, bankR.get(4)]
                        elif hh == 1 and first:
                            dep = [bankR.get(5)]
                        else:
                            dep = []
                        last = (asrc is aoTl)
                        mm = p.add("tensor",
                                   lambda e, s=s, hh=hh, asrc=asrc, wsrc=wsrc, first=first, last=last: e.matmul(
                                       PS[2][:, hh * 512:(hh + 1) * 512],
                                       asrc[:, s * 128:(s + 1) * 128],
                                       wsrc[:, hh * 512:(hh + 1) * 512],
                                       start=first, stop=last),
                                   deps=dep, tag="mmpr%d_%d_%d" % (hh, n, s))
                        first = False
                    if hh == 0:
                        mm0 = mm
                    else:
                        mm1 = mm
                sb1 = SB1(s)
                o_res = p.add("vector",
                              lambda e, s=s, sb1=sb1, nbuf=nbuf: e.scalar_tensor_tensor(
                                  x1p(n, s), nbuf[:, s * D:(s + 1) * D], 2.0,
                                  PS[2][:, 0:1024], ALU.mult, ALU.add,
                                  accum_out=stats[:, sb1:sb1 + 1]),
                              deps=[mm0, mm1, x1_free.get((n % 2, s))],
                              tag="res1_%d_%d" % (n, s))
                bankR[4] = o_res
                bankR[5] = o_res
                res1_done[(n, s)] = o_res
                st[("res", s)] = o_res
                if s == NSUB - 1 and (n + 2) < NCH:
                    buf2 = xA if (n + 2) % 2 == 0 else xB
                    x_load_ops[n + 2] = dma(
                        "gpsimd", buf2.ap().rearrange("p (s2 d) -> p s2 d", s2=NSUB),
                        x_h[:, (n + 2) * NSUB:(n + 3) * NSUB, :], deps=[o_res],
                        tag="ldx%d" % (n + 2))

            def t_ln1(s):
                sb1 = SB1(s)
                o_nm = p.add("vector",
                             lambda e, sb1=sb1: e.tensor_scalar(
                                 stats[:, sb1 + 2:sb1 + 3], stats[:, sb1:sb1 + 1],
                                 -1.0 / D, None, ALU.mult),
                             deps=[st[("res", s)]], tag="nm1_%d_%d" % (n, s))
                o_var = p.add("vector",
                              lambda e, s=s, sb1=sb1: e.scalar_tensor_tensor(
                                  junk[:, :], x1p(n, s),
                                  stats[:, sb1 + 2:sb1 + 3], x1p(n, s),
                                  ALU.add, ALU.mult,
                                  accum_out=stats[:, sb1 + 3:sb1 + 4]),
                              deps=[], tag="var1_%d_%d" % (n, s))
                o_ve = p.add("vector",
                             lambda e, sb1=sb1: e.tensor_scalar(
                                 stats[:, sb1 + 4:sb1 + 5], stats[:, sb1 + 3:sb1 + 4],
                                 1.0 / D, EPS, ALU.mult, ALU.add),
                             deps=[o_var], tag="ve1_%d_%d" % (n, s))
                o_std = p.add("scalar",
                              lambda e, sb1=sb1: e.activation(
                                  stats[:, sb1 + 5:sb1 + 6], stats[:, sb1 + 4:sb1 + 5],
                                  AF.Sqrt),
                              deps=[o_ve], tag="std1_%d_%d" % (n, s))
                o_rs = p.add("vector",
                             lambda e, sb1=sb1: e.reciprocal(
                                 stats[:, sb1 + 6:sb1 + 7], stats[:, sb1 + 5:sb1 + 6]),
                             deps=[o_std], tag="rs1_%d_%d" % (n, s))
                o_ap = p.add("vector",
                             lambda e, s=s, sb1=sb1: e.tensor_scalar(
                                 x1p(n, s), x1p(n, s),
                                 stats[:, sb1 + 2:sb1 + 3],
                                 stats[:, sb1 + 6:sb1 + 7],
                                 ALU.add, ALU.mult),
                             deps=[o_rs], tag="ap1_%d_%d" % (n, s))
                st[("ap", s)] = o_ap

            def t_x1(s):
                tr = None
                for dc in range(ND):
                    dep = [st[("ap", s)], bankR.get(4), bankR.get(5)] if dc == 0 else []
                    tr = p.add("tensor",
                               lambda e, s=s, dc=dc: e.transpose(
                                   PS[2][:, dc * 128:(dc + 1) * 128],
                                   x1p(n, s)[:, dc * 128:(dc + 1) * 128],
                                   idn[:, :]),
                               deps=dep, tag="trx1_%d_%d_%d" % (n, s, dc))
                evh = p.add("scalar",
                            lambda e, s=s: e.copy(
                                x1Th.ap().rearrange("p (dc t) -> p dc t", dc=ND)[:, :, s * 128:(s + 1) * 128],
                                PS[2].ap().rearrange("p (dc t) -> p dc t", t=128)),
                            deps=[tr], tag="evx1h_%d_%d" % (n, s))
                evl = p.add("vector",
                            lambda e, s=s: e.scalar_tensor_tensor(
                                x1Tl.ap().rearrange("p (dc t) -> p dc t", dc=ND)[:, :, s * 128:(s + 1) * 128],
                                PS[2].ap().rearrange("p (dc t) -> p dc t", t=128),
                                1.0,
                                x1Th.ap().rearrange("p (dc t) -> p dc t", dc=ND)[:, :, s * 128:(s + 1) * 128],
                                ALU.mult, ALU.subtract),
                            deps=[evh], tag="evx1l_%d_%d" % (n, s))
                bankR[4] = evl
                bankR[5] = evl
                x1T_done[n] = evl

            def t_xt(s):
                ev = emit_trx(n + 1, s)
                if s == NSUB - 1:
                    nonlocal xT_ready
                    xT_ready = ev

            thunks = [t_q, t_k, lambda: t_v(0), lambda: t_v(2), t_sc, t_kt]
            thunks += [lambda s=s: t_scan(s) for s in range(NSUB)]
            thunks += [t_ao]
            for s in range(NSUB):
                thunks.append(lambda s=s: t_proj(s))
                thunks.append(lambda s=s: t_ln1(s))
            for s in range(NSUB):
                thunks.append(lambda s=s: t_x1(s))
                if n + 1 < NCH:
                    thunks.append(lambda s=s: t_xt(s))
            return thunks

        def ffn(c, thunks):
            nonlocal prev_down_by_hc
            tq = list(thunks[:15])
            tq_late = list(thunks[15:])

            def pump(k=1):
                for _ in range(k):
                    if tq:
                        tq.pop(0)()

            def pump_late(k=1):
                for _ in range(k):
                    if tq:
                        tq.pop(0)()
                    elif tq_late:
                        tq_late.pop(0)()

            hh_by.clear()
            hlo_by.clear()
            for hc in range(NH):
                rot = hc % 2
                par = (hc // 2) % 2
                if hc % 2 == 0:
                    l_g = dma("sync",
                              wg[:, par * 2 * D:(par + 1) * 2 * D],
                              wg_s[hc // 2],
                              deps=[gate_par.get(par),
                                    cv_wg_a if hc // 2 < 3 else cv_wg_b],
                              tag="lwg%d_%d" % (c, hc))
                    l_u = dma("sync",
                              wu[:, par * 2 * D:(par + 1) * 2 * D],
                              wu_s[hc // 2],
                              deps=[up_par.get(par),
                                    cv_wu_a if hc // 2 < 3 else cv_wu_b],
                              tag="lwu%d_%d" % (c, hc))
                mm_g = None
                for dc in range(ND):
                    for xi, xsrc in ((0, x1Th), (1, x1Tl)):
                        off = par * 2 * D + dc * 256 + (hc % 2) * 128
                        dep = []
                        if dc == 0 and xi == 0:
                            dep = [l_g, x1T_done.get(c), bankR.get(2 * rot)]
                        mm_g = p.add("tensor",
                                     lambda e, off=off, dc=dc, xsrc=xsrc, rot=rot, xi=xi: e.matmul(
                                         PS[rot][:, 0:512], wg[:, off:off + 128],
                                         xsrc[:, dc * CHUNK:(dc + 1) * CHUNK],
                                         start=(dc == 0 and xi == 0),
                                         stop=(dc == ND - 1 and xi == 1)),
                                     deps=dep, tag="mmg%d_%d_%d_%d" % (c, hc, dc, xi))
                if hc % 2 == 1:
                    gate_par[par] = mm_g
                mm_u = None
                for dc in range(ND):
                    for xi, xsrc in ((0, x1Th), (1, x1Tl)):
                        off = par * 2 * D + dc * 256 + (hc % 2) * 128
                        dep = []
                        if dc == 0 and xi == 0:
                            dep = [l_u, bankR.get(2 * rot + 1)]
                        mm_u = p.add("tensor",
                                     lambda e, off=off, dc=dc, xsrc=xsrc, rot=rot, xi=xi: e.matmul(
                                         PS[rot][:, 512:1024], wu[:, off:off + 128],
                                         xsrc[:, dc * CHUNK:(dc + 1) * CHUNK],
                                         start=(dc == 0 and xi == 0),
                                         stop=(dc == ND - 1 and xi == 1)),
                                     deps=dep, tag="mmu%d_%d_%d_%d" % (c, hc, dc, xi))
                if hc % 2 == 1:
                    up_par[par] = mm_u
                spar = hc % 2
                o_si = p.add("scalar",
                             lambda e, spar=spar, rot=rot: e.activation(
                                 sg[:, 0:CHUNK],
                                 PS[rot][:, 0:512], AF.Silu, scale=float(QSC)),
                             deps=[mm_g, prev_hmul.get(spar), ms_cst],
                             tag="silu%d_%d" % (c, hc))
                bankR[2 * rot] = o_si
                o_h32 = p.add("vector",
                              lambda e, rot=rot: e.scalar_tensor_tensor(
                                  htmp[:, 0:CHUNK], PS[rot][:, 512:1024], float(QSC),
                                  sg[:, 0:CHUNK], ALU.mult, ALU.mult),
                              deps=[mm_u, o_si, prev_down_by_hc.get(hc),
                                    prev_hh.get(0), prev_hh.get(1)],
                              tag="hmul%d_%d" % (c, hc))
                bankR[2 * rot + 1] = o_h32
                o_hh = p.add("scalar",
                             lambda e, hc=hc: e.copy(
                                 hTh[:, hc * CHUNK:(hc + 1) * CHUNK],
                                 htmp[:, 0:CHUNK]),
                             deps=[o_h32], tag="hhi%d_%d" % (c, hc))
                prev_hh[0] = o_hh
                hh_by[hc] = o_hh
                o_h = p.add("gpsimd",
                            lambda e, hc=hc: e.tensor_tensor(
                                hTl[:, hc * CHUNK:(hc + 1) * CHUNK],
                                htmp[:, 0:CHUNK],
                                hTh[:, hc * CHUNK:(hc + 1) * CHUNK],
                                ALU.subtract),
                            deps=[o_hh], tag="hlo%d_%d" % (c, hc))
                prev_hmul[spar] = o_h
                prev_hh[1] = o_h
                hlo_by[hc] = o_h
                pump(1)

            down_by_hc = {}
            res2h = {}
            for half in range(2):
                for hc in range(NH):
                    pi = hc // 2
                    par = pi % 2
                    if hc % 2 == 0:
                        l_d = dma("sync", wd[:, par * 1024:(par + 1) * 1024],
                                  wd_s[half, pi],
                                  deps=[wd_par.get(par), cv_wd],
                                  tag="lwd%d_%d_%d" % (c, half, hc))
                    pbase = 2 * half
                    mmd = None
                    for s in range(NSUB):
                        for xi, hsrc in ((0, hTh), (1, hTl)):
                            dep = []
                            if s == 0 and xi == 0:
                                dep = [l_d, hh_by.get(hc), hlo_by.get(hc)]
                                if hc == 0 and half == 0:
                                    dep += [bankR.get(i) for i in range(4)]
                                if hc == 0 and half == 1:
                                    dep += [bankR.get(i) for i in range(4, 8)]
                            mmd = p.add("tensor",
                                        lambda e, hc=hc, s=s, par=par, hsrc=hsrc, xi=xi, pbase=pbase: e.matmul(
                                            PS[pbase + s // 2][:, (s % 2) * 512:(s % 2 + 1) * 512],
                                            hsrc[:, hc * CHUNK + s * 128:hc * CHUNK + (s + 1) * 128],
                                            wd[:, par * 1024 + (hc % 2) * 512:par * 1024 + (hc % 2) * 512 + 512],
                                            start=(hc == 0 and xi == 0),
                                            stop=(hc == NH - 1 and xi == 1)),
                                        deps=dep, tag="mmd%d_%d_%d_%d_%d" % (c, half, hc, s, xi))
                    if hc % 2 == 1:
                        wd_par[par] = mmd
                    if half == 1:
                        down_by_hc[hc] = mmd
                    if half == 0:
                        pump_late(1)
                for s in range(NSUB):
                    sb2 = SB2(s)
                    o_r = p.add("vector",
                                lambda e, s=s, half=half, sb2=sb2, pbase=pbase: e.scalar_tensor_tensor(
                                    x1ph(c, s, half),
                                    PS[pbase + s // 2][:, (s % 2) * 512:(s % 2 + 1) * 512],
                                    float(QSC), x1ph(c, s, half), ALU.mult, ALU.add,
                                    accum_out=stats[:, sb2 + half:sb2 + half + 1]),
                                deps=[mmd], tag="res2_%d_%d_%d" % (c, half, s))
                    res2h[(half, s)] = o_r
                    bankR[2 * half * 2 + 2 * (s // 2) + (s % 2)] = o_r
            prev_down_by_hc = down_by_hc

            for s in range(NSUB):
                sb2 = SB2(s)
                o_c0 = p.add("vector",
                             lambda e, sb2=sb2: e.tensor_tensor(
                                 stats[:, sb2:sb2 + 1], stats[:, sb2:sb2 + 1],
                                 stats[:, sb2 + 1:sb2 + 2], ALU.add),
                             deps=[res2h[(1, s)]], tag="c0_%d_%d" % (c, s))
                o_nm = p.add("vector",
                             lambda e, sb2=sb2: e.tensor_scalar(
                                 stats[:, sb2 + 2:sb2 + 3], stats[:, sb2:sb2 + 1],
                                 -1.0 / D, None, ALU.mult),
                             deps=[], tag="nm2_%d_%d" % (c, s))
                o_var = p.add("vector",
                              lambda e, s=s, sb2=sb2: e.scalar_tensor_tensor(
                                  junk[:, :], x1p(c, s),
                                  stats[:, sb2 + 2:sb2 + 3], x1p(c, s),
                                  ALU.add, ALU.mult,
                                  accum_out=stats[:, sb2 + 3:sb2 + 4]),
                              deps=[], tag="var2_%d_%d" % (c, s))
                o_ve = p.add("vector",
                             lambda e, sb2=sb2: e.tensor_scalar(
                                 stats[:, sb2 + 4:sb2 + 5], stats[:, sb2 + 3:sb2 + 4],
                                 1.0 / D, EPS, ALU.mult, ALU.add),
                             deps=[o_var], tag="ve2_%d_%d" % (c, s))
                o_std = p.add("scalar",
                              lambda e, sb2=sb2: e.activation(
                                  stats[:, sb2 + 5:sb2 + 6], stats[:, sb2 + 4:sb2 + 5],
                                  AF.Sqrt),
                              deps=[o_ve], tag="std2_%d_%d" % (c, s))
                o_rs = p.add("vector",
                             lambda e, sb2=sb2: e.reciprocal(
                                 stats[:, sb2 + 6:sb2 + 7], stats[:, sb2 + 5:sb2 + 6]),
                             deps=[o_std], tag="rs2_%d_%d" % (c, s))
                o_ap = p.add("vector",
                             lambda e, s=s, sb2=sb2: e.tensor_scalar(
                                 x1p(c, s), x1p(c, s),
                                 stats[:, sb2 + 2:sb2 + 3],
                                 stats[:, sb2 + 6:sb2 + 7],
                                 ALU.add, ALU.mult),
                             deps=[o_rs], tag="ap2_%d_%d" % (c, s))
                ci = c * NSUB + s
                o_st = dma("gpsimd", out_h[:, ci:ci + 1, :],
                           x1p(c, s).rearrange("p (one d) -> p one d", one=1),
                           deps=[o_ap], tag="stx%d_%d" % (c, s))
                store_ops.append(o_st)
                x1_free[(c % 2, s)] = o_st
            while tq or tq_late:
                pump_late(1)

        ms_cst = p.add("vector", lambda e: e.memset(stats[0:1, 63:64], float(QSC)), tag="ms_cst")
        ms_st = p.add("vector", lambda e: e.memset(stt2[:, :], 0.0), tag="ms_st")
        ms_ao1 = p.add("vector", lambda e: e.memset(aoTh[64:65, :], 1.0), tag="ms_ao1")
        ms_ao0 = p.add("vector", lambda e: e.memset(aoTl[64:65, :], 0.0), tag="ms_ao0")
        op_drow = p.add("scalar",
                        lambda e: e.activation(ktok4[0:1, 0:64], ktok4[0:1, 131:195], AF.Sigmoid),
                        deps=[ld_dec], tag="sigmoid")
        tr_dcol = p.add("tensor",
                        lambda e: e.transpose(PS[0][0:64, 0:1], ktok4[0:1, 0:64],
                                              idn[0:1, 0:1]),
                        deps=[op_drow, ld_idn], tag="tr_dcol")
        ev_dcol = p.add("vector", lambda e: e.tensor_copy(ktok4[0:64, 64:65], PS[0][0:64, 0:1]),
                        deps=[tr_dcol], tag="ev_dcol")
        op_dinvc = p.add("vector", lambda e: e.reciprocal(ktok4[0:64, 65:66], ktok4[0:64, 64:65]),
                         deps=[ev_dcol], tag="dinv_col")
        ms_d0 = p.add("vector", lambda e: e.memset(d0ext[0:64, 0:1], 1.0), tag="ms_d0")
        ms_di = p.add("vector", lambda e: e.memset(dinvT[0:64, 0:1], 1.0), tag="ms_di")
        last = op_dinvc
        for k in range(8):
            w = 1 << k
            o1 = p.add("vector",
                       lambda e, w=w: e.tensor_tensor(ktok4[0:64, 66:67], d0ext[0:64, w - 1:w],
                                                      ktok4[0:64, 64:65], ALU.mult),
                       deps=[last, ms_d0], tag="d0sc%d" % k)
            o2 = p.add("vector",
                       lambda e, w=w: e.tensor_scalar(d0ext[0:64, w:2 * w],
                                                      d0ext[0:64, 0:w],
                                                      ktok4[0:64, 66:67], None, ALU.mult),
                       deps=[o1], tag="d0dbl%d" % k)
            last = o2
        d0_done = last
        for k in range(7):
            w = 1 << k
            o1 = p.add("vector",
                       lambda e, w=w: e.tensor_tensor(ktok4[0:64, 66:67], dinvT[0:64, w - 1:w],
                                                      ktok4[0:64, 65:66], ALU.mult),
                       deps=[last, ms_di], tag="disc%d" % k)
            o2 = p.add("vector",
                       lambda e, w=w: e.tensor_scalar(dinvT[0:64, w:2 * w],
                                                      dinvT[0:64, 0:w],
                                                      ktok4[0:64, 66:67], None, ALU.mult),
                       deps=[o1], tag="didbl%d" % k)
            last = o2
        di_done = last
        tr_di = p.add("tensor",
                      lambda e: e.transpose(PS[0][:, 0:64], dinvT[0:64, 0:128],
                                            idn[0:64, 0:64]),
                      deps=[di_done, ev_dcol], tag="tr_dinv")
        ev_di = []
        for s in range(NSUB):
            ev_di.append(p.add("vector",
                               lambda e, s=s: e.tensor_copy(
                                   dinv4[:, s * 64:(s + 1) * 64], PS[0][:, 0:64]),
                               deps=[tr_di], tag="ev_di%d" % s))
        ev_d04 = [p.add("vector",
                        lambda e: e.tensor_copy(d0t4[:, :], d0ext[0:64, 0:128]),
                        deps=[d0_done], tag="d0t4")]

        def bcast(name, dst, col_slice, deps):
            trc = p.add("tensor",
                        lambda e: e.transpose(PS[0][0:1, 64:128], col_slice,
                                              idn[0:64, 0:64]),
                        deps=deps, tag=name + "_trc")
            evr = p.add("vector", lambda e: e.tensor_copy(ktok4[0:1, 67:131], PS[0][0:1, 64:128]),
                        deps=[trc], tag=name + "_evr")
            mm = p.add("tensor",
                       lambda e: e.matmul(PS[0][0:64, 128:192], ones[0:1, 0:64],
                                          ktok4[0:1, 67:131], start=True, stop=True),
                       deps=[evr, ld_mask], tag=name + "_mm")
            ev = p.add("vector", lambda e: e.tensor_copy(dst[:, :], PS[0][0:64, 128:192]),
                       deps=[mm], tag=name + "_ev")
            return ev

        bc_c = bcast("rowC", rowC, d0ext[0:64, 128:129], [d0_done, ev_dcol] + ev_di)
        bankR[0] = bc_c
        bankR[1] = bc_c

        for s in range(NSUB):
            ev = emit_trx(0, s)
        xT_ready = ev
        sw_wq = split_w("wq", wqh, wql, 128, ND * DS, 5120,
                        lambda st: dma("sync", st, wqT_h, tag="ld_wq"))
        sw_wk = split_w("wk", wkh, wkl, 128, ND * DS, 5632,
                        lambda st: dma("sync", st, wkT_h, tag="ld_wk"))
        sw_wv = split_w("wv", wvh, wvl, 128, ND * DS, 6144,
                        lambda st: dma("sync", st, wvT_h, tag="ld_wv"))
        sw_wo = split_w("wo", woh, wol, 65, D, 4096,
                        lambda st: dma("sync", st, wo_h, tag="ld_wo"))

        for t in stage_attn(0):
            t()
        for c in range(NCH):
            thunks = stage_attn(c + 1) if c + 1 < NCH else []
            ffn(c, thunks)

        p.add("vector", lambda e: e.memset(ktok4[0:64, 66:67], 0.0),
              deps=store_ops[-Prog.NSLOTS:], tag="final_wait")
        p.emit(block, sems)
    return nc


def _tile_w_test(w):
    wt = np.asarray(w, dtype=np.float32).T.reshape(ND, 128, DS)
    return np.ascontiguousarray(wt.transpose(1, 0, 2).reshape(128, ND * DS))


def _tile_gu_test(w):
    wt = np.asarray(w, dtype=np.int32).T
    wt = wt.reshape(8, 128, NH // 2, 2, 128)
    return np.ascontiguousarray(wt.transpose(2, 1, 0, 3, 4).reshape(NH // 2, 128, 16 * 128))


def kernel(L0_attn_x, L0_attn_Wq, L0_attn_bq, L0_attn_Wk, L0_attn_bk,
           L0_attn_Wv, L0_attn_bv, L0_attn_decay, L0_attn_Wo, L0_attn_bo,
           L0_ffn_w_gate, L0_ffn_w_up, L0_ffn_w_down):
    x = np.ascontiguousarray(np.asarray(L0_attn_x, dtype=np.float32))
    def _tile_w(w):
        wt = np.asarray(w, dtype=np.float32).T.reshape(ND, 128, DS)
        return np.ascontiguousarray(wt.transpose(1, 0, 2).reshape(128, ND * DS))

    wqT = _tile_w(L0_attn_Wq)
    wkT = _tile_w(L0_attn_Wk)
    wvT = _tile_w(L0_attn_Wv)
    wo_pad = np.ascontiguousarray(
        np.concatenate([np.asarray(L0_attn_Wo, dtype=np.float32).T,
                        np.asarray(L0_attn_bo, dtype=np.float32)[None, :]], axis=0))
    def _tile_gu(w):
        wt = np.asarray(w, dtype=np.int32).T
        wt = wt.reshape(8, 128, NH // 2, 2, 128)
        return np.ascontiguousarray(wt.transpose(2, 1, 0, 3, 4).reshape(NH // 2, 128, 16 * 128))

    wgT = _tile_gu(L0_ffn_w_gate)
    wuT = _tile_gu(L0_ffn_w_up)
    wdT = np.ascontiguousarray(
        np.asarray(L0_ffn_w_down, dtype=np.int32).T
        .reshape(NH // 2, 2, 128, 2, 512).transpose(3, 0, 2, 1, 4)
        .reshape(2, NH // 2, 128, 1024))
    maskT = np.ascontiguousarray(np.tile(np.triu(np.ones((128, 128), dtype=np.float32)), (1, 4)))
    ident = np.eye(128, dtype=np.float32)

    nc = build(T)
    shared = {
        "wqT": wqT, "wkT": wkT, "wvT": wvT, "wo_pad": wo_pad,
        "bq": np.asarray(L0_attn_bq, dtype=np.float32),
        "bk": np.asarray(L0_attn_bk, dtype=np.float32),
        "bv": np.asarray(L0_attn_bv, dtype=np.float32),
        "dec": np.asarray(L0_attn_decay, dtype=np.float32),
        "wgT": wgT, "wuT": wuT, "wdT": wdT,
        "maskT": maskT, "ident": ident,
    }
    in_maps = [dict(shared, x=np.ascontiguousarray(
        x[b].reshape(T // 128, 128, D).transpose(1, 0, 2))) for b in range(B)]
    res = run_bass_kernel_spmd(nc, in_maps, core_ids=list(range(B)))
    out = np.stack([np.asarray(r["out"]).transpose(1, 0, 2).reshape(T, D)
                    for r in res.results], axis=0)
    return out.astype(np.float32)



# version 35
# speedup vs baseline: 1.0443x; 1.0443x over previous
import numpy as np
from contextlib import ExitStack

import concourse.bass as bass
import concourse.mybir as mybir
from concourse.bass_utils import run_bass_kernel_spmd

F32 = mybir.dt.float32
I32 = mybir.dt.int32
BF16 = mybir.dt.bfloat16
AF = mybir.ActivationFunctionType
ALU = mybir.AluOpType

B, T, D, DS, H = 8, 4096, 1024, 64, 3072
SUB = 128
CHUNK = 512
NSUB = CHUNK // SUB
ND = D // 128
NH = H // 128
EPS = 1e-5
QSC = 1.0 / 127.0


class Prog:

    ENGINES = ("sync", "gpsimd", "tensor", "vector", "scalar")
    NSLOTS = 8

    def __init__(self):
        self.ops = []
        self.ndma = {}

    def add(self, eng, fn, deps=(), dma=False, tag=""):
        deps = [d for d in deps if d is not None]
        op = {"eng": eng, "fn": fn, "deps": deps, "dma": dma,
              "idx": len(self.ops), "tag": tag, "cum": None, "need_inc": dma,
              "slot": None, "prewait": None}
        if dma:
            i = self.ndma.get(eng, 0)
            self.ndma[eng] = i + 1
            op["slot"] = i % self.NSLOTS
        for d in deps:
            assert d["idx"] < op["idx"], (tag, d["tag"])
        self.ops.append(op)
        return op

    def emit(self, block, sems):
        for op in self.ops:
            if op["eng"] in ("vector", "scalar") or (op["eng"] == "gpsimd" and not op["dma"]):
                op["need_inc"] = True
            for d in op["deps"]:
                if d["eng"] != op["eng"] or d["dma"]:
                    d["need_inc"] = True
        counters = {}
        slot_prev = {}
        for op in self.ops:
            if op["dma"]:
                key = (op["eng"], "d", op["slot"])
                op["prewait"] = slot_prev.get(key)
                counters[key] = counters.get(key, 0) + 16
                op["cum"] = counters[key]
                op["sem_key"] = key
                slot_prev[key] = op
            elif op["need_inc"]:
                key = (op["eng"], "c")
                counters[key] = counters.get(key, 0) + 1
                op["cum"] = counters[key]
                op["sem_key"] = key

        def run_engine(eng_name):
            chain = eng_name in ("vector", "scalar", "gpsimd")
            def body(eng):
                waited = {}
                prev = None
                def wait_for(d):
                    sk, v = d["sem_key"], d["cum"]
                    if waited.get(sk, -1) >= v:
                        return
                    eng.wait_ge(sems[sk], v)
                    waited[sk] = v
                for op in self.ops:
                    if op["eng"] != eng_name:
                        continue
                    for d in op["deps"]:
                        if d["eng"] == eng_name and not d["dma"] and not chain:
                            continue
                        wait_for(d)
                    if chain and prev is not None and not op["dma"]:
                        wait_for(prev)
                    if op["prewait"] is not None:
                        wait_for(op["prewait"])
                    if chain and not op["dma"]:
                        prev = op
                    instr = op["fn"](eng)
                    if op["need_inc"]:
                        instr.then_inc(sems[op["sem_key"]],
                                       16 if op["dma"] else 1)
            return body

        block.sync(run_engine("sync"))
        block.gpsimd(run_engine("gpsimd"))
        block.tensor(run_engine("tensor"))
        block.vector(run_engine("vector"))
        block.scalar(run_engine("scalar"))


def build(T_core):
    assert T_core % CHUNK == 0
    NCH = T_core // CHUNK
    nc = bass.Bass("TRN2", target_bir_lowering=False)

    x_h = nc.dram_tensor("x", [SUB, T_core // SUB, D], F32, kind="ExternalInput").ap()
    wqkv_h = nc.dram_tensor("wqkvT", [128, 3 * ND * DS], F32, kind="ExternalInput").ap()
    wo_h = nc.dram_tensor("wo_pad", [DS + 1, D], F32, kind="ExternalInput").ap()
    bq_h = nc.dram_tensor("bq", [DS], F32, kind="ExternalInput").ap()
    bk_h = nc.dram_tensor("bk", [DS], F32, kind="ExternalInput").ap()
    bv_h = nc.dram_tensor("bv", [DS], F32, kind="ExternalInput").ap()
    dec_h = nc.dram_tensor("dec", [DS], F32, kind="ExternalInput").ap()
    wgT_h = nc.dram_tensor("wgT", [NH // 2, 128, 16 * 128], I32, kind="ExternalInput").ap()
    wuT_h = nc.dram_tensor("wuT", [NH // 2, 128, 16 * 128], I32, kind="ExternalInput").ap()
    wdT_h = nc.dram_tensor("wdT", [2, NH // 2, 128, 1024], I32, kind="ExternalInput").ap()
    wg_s = nc.dram_tensor("wg_s", [NH // 2, 128, 16 * 128], BF16, kind="Internal").ap()
    wu_s = nc.dram_tensor("wu_s", [NH // 2, 128, 16 * 128], BF16, kind="Internal").ap()
    wd_s = nc.dram_tensor("wd_s", [2, NH // 2, 128, 1024], BF16, kind="Internal").ap()
    mask_h = nc.dram_tensor("maskT", [128, CHUNK], F32, kind="ExternalInput").ap()
    idn_h = nc.dram_tensor("ident", [128, 128], F32, kind="ExternalInput").ap()
    out_h = nc.dram_tensor("out", [SUB, T_core // SUB, D], F32, kind="ExternalOutput").ap()

    p = Prog()
    es = ExitStack()
    with es:
        sb = lambda name, shape, dt=F32: es.enter_context(
            nc.sbuf_tensor(name, shape, dt))
        xA = sb("xA", [128, NSUB * D])
        xB = sb("xB", [128, NSUB * D])
        xTh = sb("xTh", [128, ND * CHUNK], BF16)
        xTl = sb("xTl", [128, ND * CHUNK], BF16)
        x1 = sb("x1", [128, 2 * NSUB * D])
        x1Th = sb("x1Th", [128, ND * CHUNK], BF16)
        x1Tl = sb("x1Tl", [128, ND * CHUNK], BF16)
        hTh = sb("hTh", [128, NH * CHUNK], BF16)
        hTl = sb("hTl", [128, NH * CHUNK], BF16)
        htmp = sb("htmp", [128, CHUNK])
        sg = sb("sg", [128, CHUNK])
        qT = sb("qT", [64, CHUNK])
        kT = sb("kT", [64, CHUNK])
        vt = sb("vt", [128, NSUB * 64])
        ktok4 = sb("ktok4", [128, NSUB * 64])
        at4 = sb("at4", [128, CHUNK])
        aoTh = sb("aoTh", [65, CHUNK], BF16)
        aoTl = sb("aoTl", [65, CHUNK], BF16)
        stt2 = sb("stt2", [64, 128])
        tsum = sb("tsum", [64, 192])
        stats = sb("stats", [128, 64])
        wqh = sb("wqh", [128, ND * DS], BF16)
        wql = sb("wql", [128, ND * DS], BF16)
        wkh = sb("wkh", [128, ND * DS], BF16)
        wkl = sb("wkl", [128, ND * DS], BF16)
        wvh = sb("wvh", [128, ND * DS], BF16)
        wvl = sb("wvl", [128, ND * DS], BF16)
        woh = sb("woh", [65, D], BF16)
        wol = sb("wol", [65, D], BF16)
        wg = sb("wg", [128, 4 * D], BF16)
        wu = sb("wu", [128, 4 * D], BF16)
        wd = sb("wd", [128, 2 * 1024], BF16)
        junk = sb("junk", [128, D], BF16)
        bq_sb = sb("bq_sb", [64, 1])
        bk_sb = sb("bk_sb", [64, 1])
        bv_row = sb("bv_row", [1, 64])
        dinv4 = sb("dinv4", [128, NSUB * 64])
        d0t4 = sb("d0t4", [64, 128])
        dinvT = x1
        d0ext = at4
        rowC = sb("rowC", [64, 64])
        mask = sb("mask", [128, CHUNK])
        idn = sb("idn", [128, 128])
        ones = mask
        PS = [es.enter_context(nc.psum_tensor("pb%d" % i, [128, 1024], F32))
              for i in range(4)]
        sems = {}
        for e in Prog.ENGINES:
            sems[(e, "c")] = es.enter_context(nc.semaphore("sem_%s_c" % e))
        for e in ("sync", "gpsimd"):
            for sl in range(Prog.NSLOTS):
                sems[(e, "d", sl)] = es.enter_context(
                    nc.semaphore("sem_%s_d%d" % (e, sl)))
        block = es.enter_context(nc.Block())

        nc.const_aps.register(F32, float(QSC), stats[0:1, 63:64])
        bankR = {}

        def dma(eng, out_ap, in_ap, deps=(), tag=""):
            return p.add(eng, lambda e, o=out_ap, i=in_ap: e.dma_start(out=o, in_=i),
                         deps=deps, dma=True, tag=tag)

        ld_dec = dma("sync", ktok4[0:1, 131:195], dec_h[None, :], tag="ld_dec")
        ld_bq = dma("sync", bq_sb[:, :], bq_h[:, None], tag="ld_bq")
        ld_bk = dma("sync", bk_sb[:, :], bk_h[:, None], tag="ld_bk")
        ld_bv = dma("sync", bv_row[:, :], bv_h[None, :], tag="ld_bv")
        x_load_ops = {}
        for n in range(min(2, NCH)):
            buf = xA if n % 2 == 0 else xB
            x_load_ops[n] = dma(
                "gpsimd", buf.ap().rearrange("p (s d) -> p s d", s=NSUB),
                x_h[:, n * NSUB:(n + 1) * NSUB, :], tag="ldx%d" % n)
        ld_idn = dma("gpsimd", idn[:, :], idn_h, tag="ld_idn")
        ld_mask = dma("gpsimd", mask[:, :], mask_h, tag="ld_mask")
        ld_wqkv = dma("gpsimd", x1[0:128, 5120:5120 + 3 * ND * DS], wqkv_h,
                      tag="ld_wqkv")
        ld_wo = dma("gpsimd", x1[0:65, 4096:4096 + D], wo_h, tag="ld_wo")

        def split_w(name, hbuf, lbuf, rows, cols, off, ld):
            stage = x1[0:rows, off:off + cols]
            hi = p.add("scalar", lambda e: e.copy(hbuf[0:rows, 0:cols], stage),
                       deps=[ld], tag="sw_h_" + name)
            lo = p.add("vector",
                       lambda e: e.scalar_tensor_tensor(
                           lbuf[0:rows, 0:cols], stage, 1.0, hbuf[0:rows, 0:cols],
                           ALU.mult, ALU.subtract),
                       deps=[hi], tag="sw_l_" + name)
            return lo

        store_ops = []
        x1_free = {}
        res1_done = {}
        gate_par = {}
        up_par = {}
        wd_par = {}
        prev_down_by_hc = {}
        prev_hmul = {}
        prev_hh = {}
        hh_by = {}
        hlo_by = {}
        prev_mm_i2 = None
        prev_stn = None
        xT_ready = None
        x1T_done = {}
        gs_box = [0]
        SB1 = lambda s: s * 7
        SB2 = lambda s: (NSUB + s) * 7

        def x1p(n, s):
            return x1[:, ((n % 2) * NSUB + s) * D:((n % 2) * NSUB + s + 1) * D]

        def x1ph(n, s, hh):
            base = ((n % 2) * NSUB + s) * D + hh * 512
            return x1[:, base:base + 512]

        cv_g = {}
        cv_u = {}
        for gl, gh, ul, uh in ((0, 3, 0, 3), (3, 12, 3, 7), (None, None, 7, 12)):
            if gl is not None:
                g = dma("gpsimd", wg_s[gl:gh], wgT_h[gl:gh], tag="cv_wg%d" % gl)
                for i in range(gl, gh):
                    cv_g[i] = g
            u = dma("gpsimd", wu_s[ul:uh], wuT_h[ul:uh], tag="cv_wu%d" % ul)
            for i in range(ul, uh):
                cv_u[i] = u
        cv_wd = dma("gpsimd", wd_s, wdT_h, tag="cv_wd")

        def emit_trx(n, s, pi=3):
            nbuf = xA if n % 2 == 0 else xB
            tr = None
            for dc in range(ND):
                dep = [x_load_ops[n], ld_idn,
                       bankR.get(2 * pi), bankR.get(2 * pi + 1)] if dc == 0 else []
                tr = p.add("tensor",
                           lambda e, s=s, dc=dc, nbuf=nbuf, pi=pi: e.transpose(
                               PS[pi][:, dc * 128:(dc + 1) * 128],
                               nbuf[:, s * D + dc * 128:s * D + (dc + 1) * 128],
                               idn[:, :]),
                           deps=dep, tag="trx%d_%d_%d" % (n, s, dc))
            evx = p.add("scalar",
                        lambda e, s=s, pi=pi: e.copy(
                            xTh.ap().rearrange("p (dc t) -> p dc t", dc=ND)[:, :, s * 128:(s + 1) * 128],
                            PS[pi].ap().rearrange("p (dc t) -> p dc t", t=128)),
                        deps=[tr], tag="evx%d_%d" % (n, s))
            evxl = p.add("vector",
                         lambda e, s=s, pi=pi: e.scalar_tensor_tensor(
                             xTl.ap().rearrange("p (dc t) -> p dc t", dc=ND)[:, :, s * 128:(s + 1) * 128],
                             PS[pi].ap().rearrange("p (dc t) -> p dc t", t=128),
                             1.0,
                             xTh.ap().rearrange("p (dc t) -> p dc t", dc=ND)[:, :, s * 128:(s + 1) * 128],
                             ALU.mult, ALU.subtract),
                         deps=[evx], tag="evxl%d_%d" % (n, s))
            bankR[2 * pi] = evxl
            bankR[2 * pi + 1] = evxl
            return evxl

        def stage_attn(n):
            st = {}
            nbuf = xA if n % 2 == 0 else xB

            def _proj3(psl, wh, wl, dep0, tagp):
                mm = None
                k = 0
                for dc in range(ND):
                    for wsrc, xsrc in ((wh, xTh), (wh, xTl), (wl, xTh)):
                        dep = dep0 if k == 0 else []
                        mm = p.add("tensor",
                                   lambda e, dc=dc, wsrc=wsrc, xsrc=xsrc, psl=psl, k=k: e.matmul(
                                       psl, wsrc[:, dc * 64:(dc + 1) * 64],
                                       xsrc[:, dc * CHUNK:(dc + 1) * CHUNK],
                                       start=(k == 0), stop=(k == 3 * ND - 1)),
                                   deps=dep, tag=tagp + "_%d" % k)
                        k += 1
                return mm

            def t_q():
                mm = _proj3(PS[2][0:64, 0:512], wqh, wql,
                            [xT_ready, sw_wq, ld_bq, bankR.get(4)], "mmq%d" % n)
                st["ev_q"] = p.add("vector",
                                   lambda e: e.tensor_scalar(qT[:, :], PS[2][0:64, 0:512],
                                                             bq_sb[:, 0:1], None, ALU.add),
                                   deps=[mm], tag="evq%d" % n)
                bankR[4] = st["ev_q"]

            def t_k():
                mm = _proj3(PS[2][0:64, 512:1024], wkh, wkl,
                            [xT_ready, sw_wk, ld_bk, bankR.get(5)], "mmk%d" % n)
                st["ev_k"] = p.add("vector",
                                   lambda e: e.tensor_scalar(kT[:, :], PS[2][0:64, 512:1024],
                                                             bk_sb[:, 0:1], None, ALU.add),
                                   deps=[mm], tag="evk%d" % n)
                bankR[5] = st["ev_k"]

            def t_v(lo):
                mm = st.get("mm_v")
                for s in range(lo, lo + 2):
                    for dc in range(ND):
                        for pi, (xsrc, wsrc) in enumerate(((xTh, wvh), (xTh, wvl), (xTl, wvh))):
                            dep = [xT_ready, sw_wv, ld_bv, bankR.get(6)] if (s == 0 and dc == 0 and pi == 0) else []
                            mm = p.add("tensor",
                                       lambda e, s=s, dc=dc, xsrc=xsrc, wsrc=wsrc, pi=pi: e.matmul(
                                           PS[3][:, s * 64:(s + 1) * 64],
                                           xsrc[:, dc * CHUNK + s * 128: dc * CHUNK + (s + 1) * 128],
                                           wsrc[:, dc * 64:(dc + 1) * 64],
                                           start=(s == 0 and dc == 0 and pi == 0), stop=False),
                                       deps=dep, tag="mmv%d_%d_%d_%d" % (n, s, dc, pi))
                    mm = p.add("tensor",
                               lambda e, s=s: e.matmul(
                                   PS[3][:, s * 64:(s + 1) * 64], ones[0:1, 0:128],
                                   bv_row[:, :], start=False, stop=(s == NSUB - 1)),
                               deps=[], tag="mmvb%d_%d" % (n, s))
                st["mm_v"] = mm
                if lo == 2:
                    st["ev_v"] = p.add("vector",
                                       lambda e: e.tensor_tensor(vt[:, :], PS[3][:, 0:256],
                                                                 dinv4[:, :], ALU.mult),
                                       deps=[mm] + ev_di, tag="evv%d" % n)
                    bankR[6] = st["ev_v"]

            def t_sc():
                mm = None
                for s in range(NSUB):
                    dep = [st["ev_q"], st["ev_k"], bankR.get(7)] if s == 0 else []
                    mm = p.add("tensor",
                               lambda e, s=s: e.matmul(
                                   PS[3][:, 512 + s * 128:512 + (s + 1) * 128],
                                   kT[:, s * 128:(s + 1) * 128],
                                   qT[:, s * 128:(s + 1) * 128],
                                   start=(s == 0), stop=(s == NSUB - 1)),
                               deps=dep, tag="mmsc%d_%d" % (n, s))
                st["ev_at4"] = p.add("vector",
                                     lambda e: e.tensor_tensor(at4[:, :], PS[3][:, 512:1024],
                                                               mask[:, :], ALU.mult),
                                     deps=[mm, ld_mask], tag="evat%d" % n)
                bankR[7] = st["ev_at4"]

            def t_kt():
                tr = None
                for s in range(NSUB):
                    dep = [st["ev_k"], st["ev_v"], bankR.get(6)] if s == 0 else []
                    tr = p.add("tensor",
                               lambda e, s=s: e.transpose(
                                   PS[3][:, 256 + s * 64:256 + (s + 1) * 64],
                                   kT[0:64, s * 128:(s + 1) * 128],
                                   idn[0:64, 0:64]),
                               deps=dep, tag="trk%d_%d" % (n, s))
                st["ev_kt4"] = p.add("vector",
                                     lambda e: e.tensor_copy(ktok4[:, :], PS[3][:, 256:512]),
                                     deps=[tr], tag="evkt%d" % n)
                bankR[6] = st["ev_kt4"]

            def t_scan(s):
                nonlocal prev_mm_i2, prev_stn
                gs = gs_box[0]
                cur = gs % 2
                nxt = (gs + 1) % 2
                gs_box[0] += 1
                mm_i1 = p.add("tensor",
                              lambda e, s=s: e.matmul(
                                  PS[2][0:64, s * 128:(s + 1) * 128],
                                  vt[:, s * 64:(s + 1) * 64],
                                  at4[:, s * 128:(s + 1) * 128],
                                  start=(s == 0), stop=False),
                              deps=[st["ev_v"], st["ev_at4"]] + ([bankR.get(4)] if s == 0 else []),
                              tag="mmi1%d_%d" % (n, s))
                mm_i2 = p.add("tensor",
                              lambda e, s=s, cur=cur: e.matmul(
                                  PS[2][0:64, s * 128:(s + 1) * 128],
                                  stt2[:, cur * 64:(cur + 1) * 64],
                                  qT[:, s * 128:(s + 1) * 128],
                                  start=False, stop=(s == NSUB - 1)),
                              deps=[prev_stn, ms_st], tag="mmi2%d_%d" % (n, s))
                mm_p = p.add("tensor",
                             lambda e, s=s: e.matmul(
                                 PS[2][0:64, 512:576], ktok4[:, s * 64:(s + 1) * 64],
                                 vt[:, s * 64:(s + 1) * 64], start=True, stop=True),
                             deps=[st["ev_kt4"], st.get("p_rd")] +
                                  ([bankR.get(5)] if s == 0 else []),
                             tag="mmp%d_%d" % (n, s))
                o_sum = p.add("vector",
                              lambda e, cur=cur: e.tensor_tensor(
                                  tsum[:, 0:64], stt2[:, cur * 64:(cur + 1) * 64],
                                  PS[2][0:64, 512:576], ALU.add),
                              deps=[mm_p, prev_stn, ms_st], tag="tsum%d_%d" % (n, s))
                st["p_rd"] = o_sum
                bankR[5] = o_sum
                o_stn = p.add("vector",
                              lambda e, nxt=nxt: e.tensor_tensor(
                                  stt2[:, nxt * 64:(nxt + 1) * 64], tsum[:, 0:64],
                                  rowC[:, :], ALU.mult),
                              deps=[bc_c, prev_mm_i2], tag="stn%d_%d" % (n, s))
                prev_mm_i2 = mm_i2
                prev_stn = o_stn
                st["ao_mm"] = mm_i2

            def t_ao():
                ev = None
                for s in range(NSUB):
                    t0 = p.add("vector",
                               lambda e, s=s: e.tensor_tensor(
                                   tsum[:, 64:192], PS[2][0:64, s * 128:(s + 1) * 128],
                                   d0t4[:, :], ALU.mult),
                               deps=[st["ao_mm"]] + ev_d04, tag="evao%d_%d" % (n, s))
                    th = p.add("scalar",
                               lambda e, s=s: e.copy(
                                   aoTh[0:64, s * 128:(s + 1) * 128], tsum[:, 64:192]),
                               deps=[t0], tag="evaoh%d_%d" % (n, s))
                    ev = p.add("vector",
                               lambda e, s=s: e.scalar_tensor_tensor(
                                   aoTl[0:64, s * 128:(s + 1) * 128],
                                   tsum[:, 64:192], 1.0,
                                   aoTh[0:64, s * 128:(s + 1) * 128],
                                   ALU.mult, ALU.subtract),
                               deps=[th], tag="evaol%d_%d" % (n, s))
                st["ev_ao"] = ev
                bankR[4] = ev

            def t_proj(s):
                mm0 = mm1 = None
                for hh in range(2):
                    first = True
                    for asrc, wsrc in ((aoTh, woh), (aoTh, wol), (aoTl, woh)):
                        if hh == 0 and first:
                            dep = [st["ev_ao"], sw_wo, ms_ao1, ms_ao0, bankR.get(4)]
                        elif hh == 1 and first:
                            dep = [bankR.get(5)]
                        else:
                            dep = []
                        last = (asrc is aoTl)
                        mm = p.add("tensor",
                                   lambda e, s=s, hh=hh, asrc=asrc, wsrc=wsrc, first=first, last=last: e.matmul(
                                       PS[2][:, hh * 512:(hh + 1) * 512],
                                       asrc[:, s * 128:(s + 1) * 128],
                                       wsrc[:, hh * 512:(hh + 1) * 512],
                                       start=first, stop=last),
                                   deps=dep, tag="mmpr%d_%d_%d" % (hh, n, s))
                        first = False
                    if hh == 0:
                        mm0 = mm
                    else:
                        mm1 = mm
                sb1 = SB1(s)
                o_res = p.add("vector",
                              lambda e, s=s, sb1=sb1, nbuf=nbuf: e.scalar_tensor_tensor(
                                  x1p(n, s), nbuf[:, s * D:(s + 1) * D], 2.0,
                                  PS[2][:, 0:1024], ALU.mult, ALU.add,
                                  accum_out=stats[:, sb1:sb1 + 1]),
                              deps=[mm0, mm1, x1_free.get((n % 2, s))],
                              tag="res1_%d_%d" % (n, s))
                bankR[4] = o_res
                bankR[5] = o_res
                res1_done[(n, s)] = o_res
                st[("res", s)] = o_res
                if s == NSUB - 1 and (n + 2) < NCH:
                    buf2 = xA if (n + 2) % 2 == 0 else xB
                    x_load_ops[n + 2] = dma(
                        "gpsimd", buf2.ap().rearrange("p (s2 d) -> p s2 d", s2=NSUB),
                        x_h[:, (n + 2) * NSUB:(n + 3) * NSUB, :], deps=[o_res],
                        tag="ldx%d" % (n + 2))

            def t_ln1(s):
                sb1 = SB1(s)
                o_nm = p.add("vector",
                             lambda e, sb1=sb1: e.tensor_scalar(
                                 stats[:, sb1 + 2:sb1 + 3], stats[:, sb1:sb1 + 1],
                                 -1.0 / D, None, ALU.mult),
                             deps=[st[("res", s)]], tag="nm1_%d_%d" % (n, s))
                o_var = p.add("vector",
                              lambda e, s=s, sb1=sb1: e.scalar_tensor_tensor(
                                  junk[:, :], x1p(n, s),
                                  stats[:, sb1 + 2:sb1 + 3], x1p(n, s),
                                  ALU.add, ALU.mult,
                                  accum_out=stats[:, sb1 + 3:sb1 + 4]),
                              deps=[], tag="var1_%d_%d" % (n, s))
                o_ve = p.add("vector",
                             lambda e, sb1=sb1: e.tensor_scalar(
                                 stats[:, sb1 + 4:sb1 + 5], stats[:, sb1 + 3:sb1 + 4],
                                 1.0 / D, EPS, ALU.mult, ALU.add),
                             deps=[o_var], tag="ve1_%d_%d" % (n, s))
                o_std = p.add("scalar",
                              lambda e, sb1=sb1: e.activation(
                                  stats[:, sb1 + 5:sb1 + 6], stats[:, sb1 + 4:sb1 + 5],
                                  AF.Sqrt),
                              deps=[o_ve], tag="std1_%d_%d" % (n, s))
                o_rs = p.add("vector",
                             lambda e, sb1=sb1: e.reciprocal(
                                 stats[:, sb1 + 6:sb1 + 7], stats[:, sb1 + 5:sb1 + 6]),
                             deps=[o_std], tag="rs1_%d_%d" % (n, s))
                o_ap = p.add("vector",
                             lambda e, s=s, sb1=sb1: e.tensor_scalar(
                                 x1p(n, s), x1p(n, s),
                                 stats[:, sb1 + 2:sb1 + 3],
                                 stats[:, sb1 + 6:sb1 + 7],
                                 ALU.add, ALU.mult),
                             deps=[o_rs], tag="ap1_%d_%d" % (n, s))
                st[("ap", s)] = o_ap

            def t_x1(s):
                tr = None
                for dc in range(ND):
                    dep = [st[("ap", s)], bankR.get(4), bankR.get(5)] if dc == 0 else []
                    tr = p.add("tensor",
                               lambda e, s=s, dc=dc: e.transpose(
                                   PS[2][:, dc * 128:(dc + 1) * 128],
                                   x1p(n, s)[:, dc * 128:(dc + 1) * 128],
                                   idn[:, :]),
                               deps=dep, tag="trx1_%d_%d_%d" % (n, s, dc))
                evh = p.add("scalar",
                            lambda e, s=s: e.copy(
                                x1Th.ap().rearrange("p (dc t) -> p dc t", dc=ND)[:, :, s * 128:(s + 1) * 128],
                                PS[2].ap().rearrange("p (dc t) -> p dc t", t=128)),
                            deps=[tr], tag="evx1h_%d_%d" % (n, s))
                evl = p.add("vector",
                            lambda e, s=s: e.scalar_tensor_tensor(
                                x1Tl.ap().rearrange("p (dc t) -> p dc t", dc=ND)[:, :, s * 128:(s + 1) * 128],
                                PS[2].ap().rearrange("p (dc t) -> p dc t", t=128),
                                1.0,
                                x1Th.ap().rearrange("p (dc t) -> p dc t", dc=ND)[:, :, s * 128:(s + 1) * 128],
                                ALU.mult, ALU.subtract),
                            deps=[evh], tag="evx1l_%d_%d" % (n, s))
                bankR[4] = evl
                bankR[5] = evl
                x1T_done[n] = evl

            def t_xt(s):
                ev = emit_trx(n + 1, s)
                if s == NSUB - 1:
                    nonlocal xT_ready
                    xT_ready = ev

            thunks = [t_q, t_k, lambda: t_v(0), lambda: t_v(2), t_sc, t_kt]
            thunks += [lambda s=s: t_scan(s) for s in range(NSUB)]
            thunks += [t_ao]
            for s in range(NSUB):
                thunks.append(lambda s=s: t_proj(s))
                thunks.append(lambda s=s: t_ln1(s))
            for s in range(NSUB):
                thunks.append(lambda s=s: t_x1(s))
                if n + 1 < NCH:
                    thunks.append(lambda s=s: t_xt(s))
            return thunks

        def ffn(c, thunks):
            nonlocal prev_down_by_hc
            tq = list(thunks[:15])
            tq_late = list(thunks[15:])

            def pump(k=1):
                for _ in range(k):
                    if tq:
                        tq.pop(0)()

            def pump_late(k=1):
                for _ in range(k):
                    if tq:
                        tq.pop(0)()
                    elif tq_late:
                        tq_late.pop(0)()

            hh_by.clear()
            hlo_by.clear()
            for hc in range(NH):
                rot = hc % 2
                par = (hc // 2) % 2
                if hc % 2 == 0:
                    l_g = dma("sync",
                              wg[:, par * 2 * D:(par + 1) * 2 * D],
                              wg_s[hc // 2],
                              deps=[gate_par.get(par), cv_g[hc // 2]],
                              tag="lwg%d_%d" % (c, hc))
                    l_u = dma("sync",
                              wu[:, par * 2 * D:(par + 1) * 2 * D],
                              wu_s[hc // 2],
                              deps=[up_par.get(par), cv_u[hc // 2]],
                              tag="lwu%d_%d" % (c, hc))
                mm_g = None
                for dc in range(ND):
                    for xi, xsrc in ((0, x1Th), (1, x1Tl)):
                        off = par * 2 * D + dc * 256 + (hc % 2) * 128
                        dep = []
                        if dc == 0 and xi == 0:
                            dep = [l_g, x1T_done.get(c), bankR.get(2 * rot)]
                        mm_g = p.add("tensor",
                                     lambda e, off=off, dc=dc, xsrc=xsrc, rot=rot, xi=xi: e.matmul(
                                         PS[rot][:, 0:512], wg[:, off:off + 128],
                                         xsrc[:, dc * CHUNK:(dc + 1) * CHUNK],
                                         start=(dc == 0 and xi == 0),
                                         stop=(dc == ND - 1 and xi == 1)),
                                     deps=dep, tag="mmg%d_%d_%d_%d" % (c, hc, dc, xi))
                if hc % 2 == 1:
                    gate_par[par] = mm_g
                mm_u = None
                for dc in range(ND):
                    for xi, xsrc in ((0, x1Th), (1, x1Tl)):
                        off = par * 2 * D + dc * 256 + (hc % 2) * 128
                        dep = []
                        if dc == 0 and xi == 0:
                            dep = [l_u, bankR.get(2 * rot + 1)]
                        mm_u = p.add("tensor",
                                     lambda e, off=off, dc=dc, xsrc=xsrc, rot=rot, xi=xi: e.matmul(
                                         PS[rot][:, 512:1024], wu[:, off:off + 128],
                                         xsrc[:, dc * CHUNK:(dc + 1) * CHUNK],
                                         start=(dc == 0 and xi == 0),
                                         stop=(dc == ND - 1 and xi == 1)),
                                     deps=dep, tag="mmu%d_%d_%d_%d" % (c, hc, dc, xi))
                if hc % 2 == 1:
                    up_par[par] = mm_u
                spar = hc % 2
                o_si = p.add("scalar",
                             lambda e, spar=spar, rot=rot: e.activation(
                                 sg[:, 0:CHUNK],
                                 PS[rot][:, 0:512], AF.Silu, scale=float(QSC)),
                             deps=[mm_g, prev_hmul.get(spar), ms_cst],
                             tag="silu%d_%d" % (c, hc))
                bankR[2 * rot] = o_si
                o_h32 = p.add("vector",
                              lambda e, rot=rot: e.scalar_tensor_tensor(
                                  htmp[:, 0:CHUNK], PS[rot][:, 512:1024], float(QSC),
                                  sg[:, 0:CHUNK], ALU.mult, ALU.mult),
                              deps=[mm_u, o_si, prev_down_by_hc.get(hc),
                                    prev_hh.get(0), prev_hh.get(1)],
                              tag="hmul%d_%d" % (c, hc))
                bankR[2 * rot + 1] = o_h32
                o_hh = p.add("scalar",
                             lambda e, hc=hc: e.copy(
                                 hTh[:, hc * CHUNK:(hc + 1) * CHUNK],
                                 htmp[:, 0:CHUNK]),
                             deps=[o_h32], tag="hhi%d_%d" % (c, hc))
                prev_hh[0] = o_hh
                hh_by[hc] = o_hh
                o_h = p.add("gpsimd",
                            lambda e, hc=hc: e.tensor_tensor(
                                hTl[:, hc * CHUNK:(hc + 1) * CHUNK],
                                htmp[:, 0:CHUNK],
                                hTh[:, hc * CHUNK:(hc + 1) * CHUNK],
                                ALU.subtract),
                            deps=[o_hh], tag="hlo%d_%d" % (c, hc))
                prev_hmul[spar] = o_h
                prev_hh[1] = o_h
                hlo_by[hc] = o_h
                pump(1)

            down_by_hc = {}
            res2h = {}
            for half in range(2):
                for hc in range(NH):
                    pi = hc // 2
                    par = pi % 2
                    if hc % 2 == 0:
                        l_d = dma("sync", wd[:, par * 1024:(par + 1) * 1024],
                                  wd_s[half, pi],
                                  deps=[wd_par.get(par), cv_wd],
                                  tag="lwd%d_%d_%d" % (c, half, hc))
                    pbase = 2 * half
                    mmd = None
                    for s in range(NSUB):
                        for xi, hsrc in ((0, hTh), (1, hTl)):
                            dep = []
                            if s == 0 and xi == 0:
                                dep = [l_d, hh_by.get(hc), hlo_by.get(hc)]
                                if hc == 0 and half == 0:
                                    dep += [bankR.get(i) for i in range(4)]
                                if hc == 0 and half == 1:
                                    dep += [bankR.get(i) for i in range(4, 8)]
                            mmd = p.add("tensor",
                                        lambda e, hc=hc, s=s, par=par, hsrc=hsrc, xi=xi, pbase=pbase: e.matmul(
                                            PS[pbase + s // 2][:, (s % 2) * 512:(s % 2 + 1) * 512],
                                            hsrc[:, hc * CHUNK + s * 128:hc * CHUNK + (s + 1) * 128],
                                            wd[:, par * 1024 + (hc % 2) * 512:par * 1024 + (hc % 2) * 512 + 512],
                                            start=(hc == 0 and xi == 0),
                                            stop=(hc == NH - 1 and xi == 1)),
                                        deps=dep, tag="mmd%d_%d_%d_%d_%d" % (c, half, hc, s, xi))
                    if hc % 2 == 1:
                        wd_par[par] = mmd
                    if half == 1:
                        down_by_hc[hc] = mmd
                    if half == 0:
                        pump_late(1)
                for s in range(NSUB):
                    sb2 = SB2(s)
                    o_r = p.add("vector",
                                lambda e, s=s, half=half, sb2=sb2, pbase=pbase: e.scalar_tensor_tensor(
                                    x1ph(c, s, half),
                                    PS[pbase + s // 2][:, (s % 2) * 512:(s % 2 + 1) * 512],
                                    float(QSC), x1ph(c, s, half), ALU.mult, ALU.add,
                                    accum_out=stats[:, sb2 + half:sb2 + half + 1]),
                                deps=[mmd], tag="res2_%d_%d_%d" % (c, half, s))
                    res2h[(half, s)] = o_r
                    bankR[2 * half * 2 + 2 * (s // 2) + (s % 2)] = o_r
            prev_down_by_hc = down_by_hc

            for s in range(NSUB):
                sb2 = SB2(s)
                o_c0 = p.add("vector",
                             lambda e, sb2=sb2: e.tensor_tensor(
                                 stats[:, sb2:sb2 + 1], stats[:, sb2:sb2 + 1],
                                 stats[:, sb2 + 1:sb2 + 2], ALU.add),
                             deps=[res2h[(1, s)]], tag="c0_%d_%d" % (c, s))
                o_nm = p.add("vector",
                             lambda e, sb2=sb2: e.tensor_scalar(
                                 stats[:, sb2 + 2:sb2 + 3], stats[:, sb2:sb2 + 1],
                                 -1.0 / D, None, ALU.mult),
                             deps=[], tag="nm2_%d_%d" % (c, s))
                o_var = p.add("vector",
                              lambda e, s=s, sb2=sb2: e.scalar_tensor_tensor(
                                  junk[:, :], x1p(c, s),
                                  stats[:, sb2 + 2:sb2 + 3], x1p(c, s),
                                  ALU.add, ALU.mult,
                                  accum_out=stats[:, sb2 + 3:sb2 + 4]),
                              deps=[], tag="var2_%d_%d" % (c, s))
                o_ve = p.add("vector",
                             lambda e, sb2=sb2: e.tensor_scalar(
                                 stats[:, sb2 + 4:sb2 + 5], stats[:, sb2 + 3:sb2 + 4],
                                 1.0 / D, EPS, ALU.mult, ALU.add),
                             deps=[o_var], tag="ve2_%d_%d" % (c, s))
                o_std = p.add("scalar",
                              lambda e, sb2=sb2: e.activation(
                                  stats[:, sb2 + 5:sb2 + 6], stats[:, sb2 + 4:sb2 + 5],
                                  AF.Sqrt),
                              deps=[o_ve], tag="std2_%d_%d" % (c, s))
                o_rs = p.add("vector",
                             lambda e, sb2=sb2: e.reciprocal(
                                 stats[:, sb2 + 6:sb2 + 7], stats[:, sb2 + 5:sb2 + 6]),
                             deps=[o_std], tag="rs2_%d_%d" % (c, s))
                o_ap = p.add("vector",
                             lambda e, s=s, sb2=sb2: e.tensor_scalar(
                                 x1p(c, s), x1p(c, s),
                                 stats[:, sb2 + 2:sb2 + 3],
                                 stats[:, sb2 + 6:sb2 + 7],
                                 ALU.add, ALU.mult),
                             deps=[o_rs], tag="ap2_%d_%d" % (c, s))
                ci = c * NSUB + s
                o_st = dma("gpsimd", out_h[:, ci:ci + 1, :],
                           x1p(c, s).rearrange("p (one d) -> p one d", one=1),
                           deps=[o_ap], tag="stx%d_%d" % (c, s))
                store_ops.append(o_st)
                x1_free[(c % 2, s)] = o_st
            while tq or tq_late:
                pump_late(1)

        ms_cst = p.add("vector", lambda e: e.memset(stats[0:1, 63:64], float(QSC)), tag="ms_cst")
        ms_st = p.add("vector", lambda e: e.memset(stt2[:, :], 0.0), tag="ms_st")
        ms_ao1 = p.add("vector", lambda e: e.memset(aoTh[64:65, :], 1.0), tag="ms_ao1")
        ms_ao0 = p.add("vector", lambda e: e.memset(aoTl[64:65, :], 0.0), tag="ms_ao0")
        op_drow = p.add("scalar",
                        lambda e: e.activation(ktok4[0:1, 0:64], ktok4[0:1, 131:195], AF.Sigmoid),
                        deps=[ld_dec], tag="sigmoid")
        tr_dcol = p.add("tensor",
                        lambda e: e.transpose(PS[0][0:64, 0:1], ktok4[0:1, 0:64],
                                              idn[0:1, 0:1]),
                        deps=[op_drow, ld_idn], tag="tr_dcol")
        ev_dcol = p.add("vector", lambda e: e.tensor_copy(ktok4[0:64, 64:65], PS[0][0:64, 0:1]),
                        deps=[tr_dcol], tag="ev_dcol")
        op_dinvc = p.add("vector", lambda e: e.reciprocal(ktok4[0:64, 65:66], ktok4[0:64, 64:65]),
                         deps=[ev_dcol], tag="dinv_col")
        ms_d0 = p.add("vector", lambda e: e.memset(d0ext[0:64, 0:1], 1.0), tag="ms_d0")
        ms_di = p.add("vector", lambda e: e.memset(dinvT[0:64, 0:1], 1.0), tag="ms_di")
        last = op_dinvc
        for k in range(8):
            w = 1 << k
            o1 = p.add("vector",
                       lambda e, w=w: e.tensor_tensor(ktok4[0:64, 66:67], d0ext[0:64, w - 1:w],
                                                      ktok4[0:64, 64:65], ALU.mult),
                       deps=[last, ms_d0], tag="d0sc%d" % k)
            o2 = p.add("vector",
                       lambda e, w=w: e.tensor_scalar(d0ext[0:64, w:2 * w],
                                                      d0ext[0:64, 0:w],
                                                      ktok4[0:64, 66:67], None, ALU.mult),
                       deps=[o1], tag="d0dbl%d" % k)
            last = o2
        d0_done = last
        for k in range(7):
            w = 1 << k
            o1 = p.add("vector",
                       lambda e, w=w: e.tensor_tensor(ktok4[0:64, 66:67], dinvT[0:64, w - 1:w],
                                                      ktok4[0:64, 65:66], ALU.mult),
                       deps=[last, ms_di], tag="disc%d" % k)
            o2 = p.add("vector",
                       lambda e, w=w: e.tensor_scalar(dinvT[0:64, w:2 * w],
                                                      dinvT[0:64, 0:w],
                                                      ktok4[0:64, 66:67], None, ALU.mult),
                       deps=[o1], tag="didbl%d" % k)
            last = o2
        di_done = last
        tr_di = p.add("tensor",
                      lambda e: e.transpose(PS[0][:, 0:64], dinvT[0:64, 0:128],
                                            idn[0:64, 0:64]),
                      deps=[di_done, ev_dcol], tag="tr_dinv")
        ev_di = []
        for s in range(NSUB):
            ev_di.append(p.add("vector",
                               lambda e, s=s: e.tensor_copy(
                                   dinv4[:, s * 64:(s + 1) * 64], PS[0][:, 0:64]),
                               deps=[tr_di], tag="ev_di%d" % s))
        ev_d04 = [p.add("vector",
                        lambda e: e.tensor_copy(d0t4[:, :], d0ext[0:64, 0:128]),
                        deps=[d0_done], tag="d0t4")]

        def bcast(name, dst, col_slice, deps):
            trc = p.add("tensor",
                        lambda e: e.transpose(PS[0][0:1, 64:128], col_slice,
                                              idn[0:64, 0:64]),
                        deps=deps, tag=name + "_trc")
            evr = p.add("vector", lambda e: e.tensor_copy(ktok4[0:1, 67:131], PS[0][0:1, 64:128]),
                        deps=[trc], tag=name + "_evr")
            mm = p.add("tensor",
                       lambda e: e.matmul(PS[0][0:64, 128:192], ones[0:1, 0:64],
                                          ktok4[0:1, 67:131], start=True, stop=True),
                       deps=[evr, ld_mask], tag=name + "_mm")
            ev = p.add("vector", lambda e: e.tensor_copy(dst[:, :], PS[0][0:64, 128:192]),
                       deps=[mm], tag=name + "_ev")
            return ev

        bc_c = bcast("rowC", rowC, d0ext[0:64, 128:129], [d0_done, ev_dcol] + ev_di)
        bankR[0] = bc_c
        bankR[1] = bc_c

        for s in range(NSUB):
            ev = emit_trx(0, s)
        xT_ready = ev
        sw_wq = split_w("wq", wqh, wql, 128, ND * DS, 5120, ld_wqkv)
        sw_wk = split_w("wk", wkh, wkl, 128, ND * DS, 5632, ld_wqkv)
        sw_wv = split_w("wv", wvh, wvl, 128, ND * DS, 6144, ld_wqkv)
        sw_wo = split_w("wo", woh, wol, 65, D, 4096, ld_wo)

        for t in stage_attn(0):
            t()
        for c in range(NCH):
            thunks = stage_attn(c + 1) if c + 1 < NCH else []
            ffn(c, thunks)

        p.add("vector", lambda e: e.memset(ktok4[0:64, 66:67], 0.0),
              deps=store_ops[-Prog.NSLOTS:], tag="final_wait")
        p.emit(block, sems)
    return nc


def _tile_w_test(w):
    wt = np.asarray(w, dtype=np.float32).T.reshape(ND, 128, DS)
    return np.ascontiguousarray(wt.transpose(1, 0, 2).reshape(128, ND * DS))


def _tile_gu_test(w):
    wt = np.asarray(w, dtype=np.int32).T
    wt = wt.reshape(8, 128, NH // 2, 2, 128)
    return np.ascontiguousarray(wt.transpose(2, 1, 0, 3, 4).reshape(NH // 2, 128, 16 * 128))


def kernel(L0_attn_x, L0_attn_Wq, L0_attn_bq, L0_attn_Wk, L0_attn_bk,
           L0_attn_Wv, L0_attn_bv, L0_attn_decay, L0_attn_Wo, L0_attn_bo,
           L0_ffn_w_gate, L0_ffn_w_up, L0_ffn_w_down):
    x = np.ascontiguousarray(np.asarray(L0_attn_x, dtype=np.float32))
    def _tile_w(w):
        wt = np.asarray(w, dtype=np.float32).T.reshape(ND, 128, DS)
        return np.ascontiguousarray(wt.transpose(1, 0, 2).reshape(128, ND * DS))

    wqkvT = np.ascontiguousarray(np.concatenate(
        [_tile_w(L0_attn_Wq), _tile_w(L0_attn_Wk), _tile_w(L0_attn_Wv)], axis=1))
    wo_pad = np.ascontiguousarray(
        np.concatenate([np.asarray(L0_attn_Wo, dtype=np.float32).T,
                        np.asarray(L0_attn_bo, dtype=np.float32)[None, :]], axis=0))
    def _tile_gu(w):
        wt = np.asarray(w, dtype=np.int32).T
        wt = wt.reshape(8, 128, NH // 2, 2, 128)
        return np.ascontiguousarray(wt.transpose(2, 1, 0, 3, 4).reshape(NH // 2, 128, 16 * 128))

    wgT = _tile_gu(L0_ffn_w_gate)
    wuT = _tile_gu(L0_ffn_w_up)
    wdT = np.ascontiguousarray(
        np.asarray(L0_ffn_w_down, dtype=np.int32).T
        .reshape(NH // 2, 2, 128, 2, 512).transpose(3, 0, 2, 1, 4)
        .reshape(2, NH // 2, 128, 1024))
    maskT = np.ascontiguousarray(np.tile(np.triu(np.ones((128, 128), dtype=np.float32)), (1, 4)))
    ident = np.eye(128, dtype=np.float32)

    nc = build(T)
    shared = {
        "wqkvT": wqkvT, "wo_pad": wo_pad,
        "bq": np.asarray(L0_attn_bq, dtype=np.float32),
        "bk": np.asarray(L0_attn_bk, dtype=np.float32),
        "bv": np.asarray(L0_attn_bv, dtype=np.float32),
        "dec": np.asarray(L0_attn_decay, dtype=np.float32),
        "wgT": wgT, "wuT": wuT, "wdT": wdT,
        "maskT": maskT, "ident": ident,
    }
    in_maps = [dict(shared, x=np.ascontiguousarray(
        x[b].reshape(T // 128, 128, D).transpose(1, 0, 2))) for b in range(B)]
    res = run_bass_kernel_spmd(nc, in_maps, core_ids=list(range(B)))
    out = np.stack([np.asarray(r["out"]).transpose(1, 0, 2).reshape(T, D)
                    for r in res.results], axis=0)
    return out.astype(np.float32)

